# version 04
# baseline (speedup 1.0000x reference)
import numpy as np
import ml_dtypes

import concourse.bass as bass
import concourse.mybir as mybir
from concourse.bass_utils import run_bass_kernel_spmd

F32 = mybir.dt.float32
BF16 = mybir.dt.bfloat16
I32 = mybir.dt.int32
AF = mybir.ActivationFunctionType
ALU = mybir.AluOpType

P = 128
NCORES = 8
N_TOK = 1024
HALO = 3
NH = N_TOK + HALO
D_MODEL = 2048
MD = D_MODEL // P
D_INNER = 4096
DI = D_INNER // P
OC = 4
S = 16
NE = 8
DH = 1024
HM = DH // P
SH = 2048
SHM = SH // P
TSEG = 1024
NSEG = 8
T_ALL = 8192
PIECES3 = [(0, 512), (512, 512), (1024, 3)]


class Buf:

    def __init__(self, ap, name=""):
        self.ap = ap
        self.name = name
        self.w = None
        self.r = []

    def __getitem__(self, idx):
        return self.ap[idx]


class Builder:
    def __init__(self, nc, sems, pool=()):
        self.nc = nc
        self.sems = sems
        self.cnt = {k: 0 for k in sems}
        self.sched = {"sync": [], "vector": [], "scalar": [], "tensor": [], "gpsimd": []}
        self.pool = list(pool)
        self.pool_next = 0
        self.stream_map = {}

    def skey(self, key):
        if key not in self.stream_map:
            name = self.pool[self.pool_next]
            self.pool_next += 1
            self.stream_map[key] = name
        return self.stream_map[key]

    def tick(self, sem, k=1):
        self.cnt[sem] += k
        return (sem, self.cnt[sem])

    def on(self, eng, fn):
        self.sched[eng].append(fn)

    def wait_toks(self, toks):
        toks = [t for t in toks if t is not None]
        best = {}
        for s, v in toks:
            best[s] = max(best.get(s, 0), v)
        sems = self.sems

        def emit(eng):
            for s, v in best.items():
                if v > 0:
                    eng.wait_ge(sems[s], v)
        return emit

    def pre_write(self, *bufs):
        toks = []
        for b in bufs:
            toks.extend(b.r)
            if b.w is not None:
                toks.append(b.w)
        return toks

    def post_write(self, tok, *bufs):
        for b in bufs:
            b.w = tok
            b.r = []

    def pre_read(self, *bufs):
        return [b.w for b in bufs if b.w is not None]

    def post_read(self, tok, *bufs):
        for b in bufs:
            b.r.append(tok)


def build_program():
    nc = bass.Bass("TRN2", target_bir_lowering=False)

    x_t = nc.dram_tensor("x_t", [D_MODEL, NH], F32, kind="ExternalInput").ap()
    halo_mask = nc.dram_tensor("halo_mask", [P, 1], F32, kind="ExternalInput").ap()
    rank_h = nc.dram_tensor("rank", [1, 1], I32, kind="ExternalInput").ap()
    eye_h = nc.dram_tensor("eye", [P, P], F32, kind="ExternalInput").ap()
    w_in = nc.dram_tensor("w_in", [D_MODEL, 2 * D_INNER], F32, kind="ExternalInput").ap()
    w_xp = nc.dram_tensor("w_xp", [D_INNER, 160], F32, kind="ExternalInput").ap()
    w_dt = nc.dram_tensor("w_dt", [128, 512], F32, kind="ExternalInput").ap()
    dtb_h = nc.dram_tensor("dt_b", [P, OC], F32, kind="ExternalInput").ap()
    A_h = nc.dram_tensor("A_sl", [P, OC * S], F32, kind="ExternalInput").ap()
    w_out = nc.dram_tensor("w_out", [D_INNER, D_MODEL], F32, kind="ExternalInput").ap()
    convw_h = nc.dram_tensor("conv_w", [P, DI * 4], F32, kind="ExternalInput").ap()
    convb_h = nc.dram_tensor("conv_b", [P, DI], F32, kind="ExternalInput").ap()
    D_h = nc.dram_tensor("D_v", [P, DI], F32, kind="ExternalInput").ap()
    gw_h = nc.dram_tensor("gate_w", [D_MODEL, NE], F32, kind="ExternalInput").ap()
    w1_h = nc.dram_tensor("w1", [NE, D_MODEL, DH], F32, kind="ExternalInput").ap()
    b1_h = nc.dram_tensor("b1", [P, NE * HM], F32, kind="ExternalInput").ap()
    w2_h = nc.dram_tensor("w2", [NE, DH, D_MODEL], F32, kind="ExternalInput").ap()
    b2_h = nc.dram_tensor("b2", [NE, D_MODEL], F32, kind="ExternalInput").ap()
    sw1_h = nc.dram_tensor("sw1", [D_MODEL, SH], F32, kind="ExternalInput").ap()
    sb1_h = nc.dram_tensor("sb1", [P, SHM], F32, kind="ExternalInput").ap()
    sw2_h = nc.dram_tensor("sw2", [SH, D_MODEL], F32, kind="ExternalInput").ap()
    sb2_h = nc.dram_tensor("sb2", [P, MD], F32, kind="ExternalInput").ap()

    out_t = nc.dram_tensor("out_t", [D_MODEL, N_TOK], F32, kind="ExternalOutput").ap()

    xt_d = nc.dram_tensor("xt_d", [D_INNER, N_TOK], BF16)
    ssm_d = nc.dram_tensor("ssm_d", [160, N_TOK], BF16)
    gxt = nc.dram_tensor("gxt", [NCORES * D_INNER, N_TOK], BF16)
    gssm = nc.dram_tensor("gssm", [NCORES * 160, N_TOK], BF16)
    y_d = nc.dram_tensor("y_d", [512, T_ALL], BF16)
    gy = nc.dram_tensor("gy", [NCORES * 512, T_ALL], BF16)
    gate_d = nc.dram_tensor("gate_d", [D_INNER, N_TOK], BF16)
    gates_dr = nc.dram_tensor("gates_dr", [NE, N_TOK], BF16)
    ffn_d = nc.dram_tensor("ffn_d", [D_MODEL, N_TOK], BF16)

    import contextlib
    with contextlib.ExitStack() as _st:
        def _e(x):
            return _st.enter_context(x)
        AR1 = _e(nc.sbuf_tensor("AR1", [P, 16 * NH], BF16))
        AR2 = _e(nc.sbuf_tensor("AR2", [P, 16 * N_TOK], BF16))
        AR3 = _e(nc.sbuf_tensor("AR3", [P, 8 * N_TOK], BF16))
        AR4 = _e(nc.sbuf_tensor("AR4", [P, 8 * N_TOK], BF16))
        ARF = _e(nc.sbuf_tensor("ARF", [P, 6 * NH], F32))
        WS = _e(nc.sbuf_tensor("WS", [P, 16 * 512], BF16))
        SML = _e(nc.sbuf_tensor("SML", [P, 512], F32))
        LOWF = _e(nc.sbuf_tensor("LOWF", [32, 5200], F32))
        LOWB = _e(nc.sbuf_tensor("LOWB", [32, 2048], BF16))
        PRM = _e(nc.sbuf_tensor("PRM", [P, 640], F32))
        PRB = _e(nc.sbuf_tensor("PRB", [P, 640], BF16))
        RNK = _e(nc.sbuf_tensor("RNK", [1, 1], I32))
        B2S = _e(nc.sbuf_tensor("B2S", [NE, D_MODEL], BF16))
        psA = _e(nc.psum_tensor("psA", [P, 512], F32))
        psB = _e(nc.psum_tensor("psB", [P, 512], F32))
        psC = _e(nc.psum_tensor("psC", [P, 512], F32))
        psD = _e(nc.psum_tensor("psD", [P, 512], F32))
        psS1 = _e(nc.psum_tensor("psS1", [P, N_TOK], F32))
        psS2 = _e(nc.psum_tensor("psS2", [32, N_TOK], F32))
        dW = _e(nc.semaphore("dW"))
        dX = _e(nc.semaphore("dX"))
        vC = _e(nc.semaphore("vC"))
        aC = _e(nc.semaphore("aC"))
        tC = _e(nc.semaphore("tC"))
        gC = _e(nc.semaphore("gC"))
        NPOOL = 72
        pool_handles = {f"p{i}": _e(nc.semaphore(f"p{i}")) for i in range(NPOOL)}
        block = _e(nc.Block())

        sems = {"dW": dW, "dX": dX, "vC": vC, "aC": aC, "tC": tC, "gC": gC}
        sems.update(pool_handles)
        B = Builder(nc, sems, pool=[f"p{i}" for i in range(NPOOL)])

        xn = [Buf(AR1[:, md * NH:(md + 1) * NH], f"xn{md}") for md in range(MD)]
        state = [Buf(AR1[:, s * TSEG:(s + 1) * TSEG], f"st{s}") for s in range(S)]
        ym = [Buf(AR1[:, dc * 512:(dc + 1) * 512], f"ym{dc}") for dc in range(DI)]
        AR2s = [Buf(AR2[:, i * N_TOK:(i + 1) * N_TOK], f"AR2_{i}") for i in range(16)]
        routed = AR2s
        gbc = [Buf(AR3[:, e * N_TOK:(e + 1) * N_TOK], f"gbc{e}") for e in range(NE)]
        hch = [Buf(AR4[:, hm * N_TOK:(hm + 1) * N_TOK], f"h{hm}") for hm in range(HM)]
        shch = [Buf(AR4[:, hm * 512:(hm + 1) * 512], f"sh{hm}") for hm in range(SHM)]
        xf = [Buf(ARF[:, i * NH:(i + 1) * NH], f"xf{i}") for i in range(2)]
        x2f = Buf(ARF[:, 2 * NH:3 * NH], "x2f")
        rstd_b = Buf(ARF[:, 3 * NH:4 * NH], "rstd_b")
        mr_b = Buf(ARF[:, 4 * NH:5 * NH], "mr_b")
        txf = Buf(ARF[:, 5 * NH:6 * NH], "txf")
        spA = Buf(ARF[:, 0 * TSEG:1 * TSEG], "spA")
        spB = Buf(ARF[:, 1 * TSEG:2 * TSEG], "spB")
        spC = Buf(ARF[:, 2 * TSEG:3 * TSEG], "spC")
        dt_f = Buf(ARF[:, 3 * TSEG:4 * TSEG], "dt_f")
        xre = [Buf(ARF[:, i * 512:(i + 1) * 512], f"xre{i}") for i in range(2)]
        oev = [Buf(ARF[:, (2 + i) * 512:(3 + i) * 512], f"oev{i}") for i in range(2)]
        yfl = [Buf(ARF[:, (4 + i) * 512:(5 + i) * 512], f"yfl{i}") for i in range(2)]

        NWS = 16
        WSLOT = 512
        ws_buf = [Buf(None, f"ws{i}") for i in range(NWS)]
        ws_ptr = [0]

        gtok = Buf(SML[:, 0:8], "gtok")
        mask1 = Buf(SML[:, 8:16], "mask1")
        l2s = Buf(SML[:, 16:24], "l2s")
        gasm = Buf(SML[:, 24:32], "gasm")
        m1 = Buf(SML[:, 32:33], "m1")
        m2 = Buf(SML[:, 33:34], "m2")
        g1 = Buf(SML[:, 34:35], "g1")
        g2 = Buf(SML[:, 35:36], "g2")
        d12 = Buf(SML[:, 36:37], "d12")
        ones_f = Buf(SML[:, 40:41], "ones_f")
        onesr = Buf(SML[:, 48:176], "onesr")
        carry = Buf(SML[:, 256:320], "carry")
        eye_f = Buf(SML[:, 320:448], "eye_f")
        glog = Buf(LOWF[0:8, 0:NH], "glog")
        gatesT = Buf(LOWF[0:8, NH:NH + N_TOK], "gatesT")
        stat_m = Buf(LOWF[0:1, 2100:2100 + NH], "stat_m")
        stat_q = Buf(LOWF[0:1, 3140:3140 + NH], "stat_q")
        stat_r = Buf(LOWF[0:1, 4170:4170 + NH], "stat_r")
        ssm32 = Buf(LOWB[0:32, 0:N_TOK], "ssm32")
        gatesTb = Buf(LOWB[0:8, N_TOK:2 * N_TOK], "gatesTb")
        convw_p = Buf(PRM[:, 0:DI * 4], "convw")
        convb_p = Buf(PRM[:, 128:128 + DI], "convb")
        D_p = Buf(PRM[:, 160:160 + DI], "D_p")
        A_p = Buf(PRM[:, 192:192 + OC * S], "A_p")
        dtb_p = Buf(PRM[:, 256:256 + OC], "dtb")
        b1_p = Buf(PRM[:, 260:260 + NE * HM], "b1")
        sb1_p = Buf(PRM[:, 324:324 + SHM], "sb1")
        sb2_p = Buf(PRM[:, 340:340 + MD], "sb2")
        hmask_p = Buf(PRM[:, 356:357], "hmask")
        gw_p = Buf(PRM[:, 384:384 + MD * NE], "gw")
        eye_b = Buf(PRB[:, 0:128], "eye_b")
        wdt_p = Buf(PRB[:, 128:128 + 512], "wdt")
        rnk_b = Buf(RNK[:, :], "rnk")
        b2_b = Buf(B2S[:, :], "b2")
        pA = Buf(psA[:, :], "pA")
        pB = Buf(psB[:, :], "pB")
        pC = Buf(psC[:, :], "pC")
        pD = Buf(psD[:, :], "pD")
        pS1 = Buf(psS1[:, :], "pS1")
        pS2 = Buf(psS2[:, :], "pS2")
        pAB = [pA, pB]
        pCD = [pC, pD]

        regbox = {}

        def sync_dma(dst_bufs, pairs, extra_waits=(), reads=()):
            kb = dst_bufs[0] if dst_bufs else (reads[0] if reads else None)
            sname = B.skey(id(kb)) if kb is not None else "dX"
            semh = B.sems[sname]
            toks = list(extra_waits)
            for b in dst_bufs:
                toks.extend(B.pre_write(b))
            toks.extend(B.pre_read(*reads))
            wfrag = B.wait_toks(toks)

            def f(sync):
                wfrag(sync)
                for o, i in pairs:
                    sync.dma_start(out=o, in_=i).then_inc(semh, 16)
            B.on("sync", f)
            tok = B.tick(sname, 16 * len(pairs))
            B.post_write(tok, *dst_bufs)
            for b in reads:
                B.post_read(tok, b)
            return tok

        def gp_dma(dst_bufs, pairs, extra_waits=(), reads=()):
            kb = dst_bufs[0] if dst_bufs else (reads[0] if reads else None)
            sname = B.skey(id(kb)) if kb is not None else "dW"
            semh = B.sems[sname]
            toks = list(extra_waits)
            for b in dst_bufs:
                toks.extend(B.pre_write(b))
            toks.extend(B.pre_read(*reads))
            wfrag = B.wait_toks(toks)

            def f(gpsimd):
                wfrag(gpsimd)
                for o, i in pairs:
                    gpsimd.dma_start(out=o, in_=i).then_inc(semh, 16)
            B.on("gpsimd", f)
            tok = B.tick(sname, 16 * len(pairs))
            B.post_write(tok, *dst_bufs)
            for b in reads:
                B.post_read(tok, b)
            return tok

        def engine_op(eng, sem, body, writes=(), reads=(), extra_waits=()):
            toks = list(extra_waits)
            for b in writes:
                toks.extend(B.pre_write(b))
            toks.extend(B.pre_read(*reads))
            wfrag = B.wait_toks(toks)
            semh = B.sems[sem]

            def f(e):
                wfrag(e)
                last = body(e)
                last.then_inc(semh, 1)
            B.on(eng, f)
            tok = B.tick(sem, 1)
            B.post_write(tok, *writes)
            for b in reads:
                B.post_read(tok, b)
            return tok

        def stream_w(src_ap, cols, nslots):
            slot = ws_ptr[0]
            if slot % nslots != 0:
                slot = ((slot + nslots - 1) // nslots) * nslots
            slot = slot % NWS
            bufs = [ws_buf[slot + k] for k in range(nslots)]
            dst = WS[:, slot * WSLOT: slot * WSLOT + cols]
            nk = src_ap.shape[1]
            dst3 = dst.rearrange("p (kc m) -> p kc m", kc=nk)
            sname = B.skey(f"ws{slot}")
            semh = B.sems[sname]
            toks = []
            for b in bufs:
                toks.extend(B.pre_write(b))
            wfrag = B.wait_toks(toks)

            def f(gpsimd):
                wfrag(gpsimd)
                gpsimd.dma_start(out=dst3, in_=src_ap).then_inc(semh, 16)
            B.on("gpsimd", f)
            tok = B.tick(sname, 16)
            for b in bufs:
                B.post_write(tok, b)
            ws_ptr[0] = (slot + nslots) % NWS
            return dst, bufs

        sync_dma([rnk_b], [(RNK[:, :], rank_h[:, :])])
        sync_dma([convw_p], [(convw_p.ap, convw_h[:, :])])
        sync_dma([convb_p], [(convb_p.ap, convb_h[:, :])])
        sync_dma([D_p], [(D_p.ap, D_h[:, :])])
        sync_dma([A_p], [(A_p.ap, A_h[:, :])])
        sync_dma([dtb_p], [(dtb_p.ap, dtb_h[:, :])])
        sync_dma([b1_p], [(b1_p.ap, b1_h[:, :])])
        sync_dma([sb1_p], [(sb1_p.ap, sb1_h[:, :])])
        sync_dma([sb2_p], [(sb2_p.ap, sb2_h[:, :])])
        sync_dma([hmask_p], [(hmask_p.ap, halo_mask[:, :])])
        sync_dma([eye_f], [(eye_f.ap, eye_h[:, :])])
        gwr = gw_h.rearrange("(c p) e -> p c e", p=P)
        sync_dma([gw_p], [(gw_p.ap.rearrange("p (c e) -> p c e", e=NE), gwr)])
        gp_dma([eye_b], [(eye_b.ap, eye_h[:, :])])
        gp_dma([wdt_p], [(wdt_p.ap, w_dt[:, :])])
        gp_dma([b2_b], [(b2_b.ap, b2_h[:, :])])

        rnk_wait = B.wait_toks(B.pre_read(rnk_b))

        def load_rank(sync):
            rnk_wait(sync)
            regbox["rank"] = nc.values_load(
                RNK[0:1, 0:1], engines=[mybir.EngineType.SP],
                min_val=0, max_val=7, skip_runtime_bounds_check=True)
        B.on("sync", load_rank)

        engine_op("vector", "vC", lambda v: v.memset(ones_f.ap, 1.0), writes=[ones_f])
        engine_op("vector", "vC", lambda v: v.memset(onesr.ap[0:1, :], 1.0), writes=[onesr])

        xr = x_t.rearrange("(c p) n -> c p n", p=P)
        for md in range(MD):
            bx = xf[md % 2]
            sync_dma([bx], [(bx.ap, xr[md, :, :])])
            engine_op("scalar", "aC",
                      lambda s, bx=bx: s.activation(out=x2f.ap, in_=bx.ap, func=AF.Square),
                      writes=[x2f], reads=[bx])

            def mmstats(t, bx=bx, md=md):
                st, sp = (md == 0), (md == MD - 1)
                k = dict(start=st, stop=sp, skip_group_check=True)
                t.matmul(out=psS1[0:1, 0:512], lhsT=ones_f.ap, rhs=bx.ap[:, 0:512], **k)
                t.matmul(out=psS1[0:1, 512:1024], lhsT=ones_f.ap, rhs=bx.ap[:, 512:1024], **k)
                t.matmul(out=psA[0:1, 0:3], lhsT=ones_f.ap, rhs=bx.ap[:, 1024:1027], **k)
                t.matmul(out=psS2[0:1, 0:512], lhsT=ones_f.ap, rhs=x2f.ap[:, 0:512], **k)
                t.matmul(out=psS2[0:1, 512:1024], lhsT=ones_f.ap, rhs=x2f.ap[:, 512:1024], **k)
                return t.matmul(out=psB[0:1, 0:3], lhsT=ones_f.ap, rhs=x2f.ap[:, 1024:1027], **k)
            wr = [pS1, pA, pS2, pB] if md == MD - 1 else []
            engine_op("tensor", "tC", mmstats, writes=wr, reads=[bx, x2f, ones_f])

        def rows1(v):
            v.tensor_scalar_mul(out=stat_m.ap[0:1, 0:512], in0=psS1[0:1, 0:512], scalar1=1.0 / D_MODEL)
            v.tensor_scalar_mul(out=stat_m.ap[0:1, 512:1024], in0=psS1[0:1, 512:1024], scalar1=1.0 / D_MODEL)
            v.tensor_scalar_mul(out=stat_m.ap[0:1, 1024:1027], in0=psA[0:1, 0:3], scalar1=1.0 / D_MODEL)
            v.tensor_scalar_mul(out=stat_q.ap[0:1, 0:512], in0=psS2[0:1, 0:512], scalar1=1.0 / D_MODEL)
            v.tensor_scalar_mul(out=stat_q.ap[0:1, 512:1024], in0=psS2[0:1, 512:1024], scalar1=1.0 / D_MODEL)
            return v.tensor_scalar_mul(out=stat_q.ap[0:1, 1024:1027], in0=psB[0:1, 0:3], scalar1=1.0 / D_MODEL)
        engine_op("vector", "vC", rows1, writes=[stat_m, stat_q],
                  reads=[pS1, pS2, pA, pB])
        engine_op("vector", "vC",
                  lambda v: v.tensor_mul(out=stat_r.ap, in0=stat_m.ap, in1=stat_m.ap),
                  writes=[stat_r], reads=[stat_m])
        engine_op("vector", "vC",
                  lambda v: v.tensor_sub(out=stat_q.ap, in0=stat_q.ap, in1=stat_r.ap),
                  writes=[stat_q], reads=[stat_q, stat_r])
        engine_op("vector", "vC",
                  lambda v: v.tensor_scalar_add(out=stat_q.ap, in0=stat_q.ap, scalar1=1e-5),
                  writes=[stat_q], reads=[stat_q])
        engine_op("scalar", "aC", lambda s: s.activation(out=stat_r.ap, in_=stat_q.ap, func=AF.Sqrt),
                  writes=[stat_r], reads=[stat_q])

        engine_op("vector", "vC",
                  lambda v: v.reciprocal(out=stat_q.ap, in_=stat_r.ap),
                  writes=[stat_q], reads=[stat_r])
        engine_op("vector", "vC",
                  lambda v: v.tensor_mul(out=stat_m.ap, in0=stat_m.ap, in1=stat_q.ap),
                  writes=[stat_m], reads=[stat_m, stat_q])

        for pi, (c0, w) in enumerate(PIECES3):
            engine_op("tensor", "tC",
                      lambda t, c0=c0, w=w: (
                          t.matmul(out=psC[:, 0:w], lhsT=onesr.ap[0:1, :],
                                   rhs=stat_q.ap[0:1, c0:c0 + w], start=True, stop=True),
                          t.matmul(out=psD[:, 0:w], lhsT=onesr.ap[0:1, :],
                                   rhs=stat_m.ap[0:1, c0:c0 + w], start=True, stop=True))[-1],
                      writes=[pC, pD], reads=[stat_q, stat_m, onesr])
            engine_op("vector", "vC",
                      lambda v, c0=c0, w=w: (
                          v.tensor_copy(out=rstd_b.ap[:, c0:c0 + w], in_=psC[:, 0:w]),
                          v.tensor_copy(out=mr_b.ap[:, c0:c0 + w], in_=psD[:, 0:w]))[-1],
                      writes=[rstd_b, mr_b] if pi == 2 else [], reads=[pC, pD])

        for md in range(MD):
            bx = xf[md % 2]
            sync_dma([bx], [(bx.ap, xr[md, :, :])])

            engine_op("vector", "vC",
                      lambda v, bx=bx: v.tensor_mul(out=txf.ap, in0=bx.ap, in1=rstd_b.ap),
                      writes=[txf], reads=[bx, rstd_b])
            engine_op("vector", "vC",
                      lambda v: v.tensor_sub(out=txf.ap, in0=txf.ap, in1=mr_b.ap),
                      writes=[txf], reads=[txf, mr_b])
            engine_op("vector", "vC",
                      lambda v: v.tensor_scalar_mul(out=txf.ap[:, 0:HALO],
                                                    in0=txf.ap[:, 0:HALO], scalar1=hmask_p.ap),
                      writes=[txf], reads=[txf, hmask_p])
            engine_op("vector", "vC",
                      lambda v, md=md: v.tensor_copy(out=xn[md].ap, in_=txf.ap),
                      writes=[xn[md]], reads=[txf])

            def gmm(t, md=md):
                st, sp = (md == 0), (md == MD - 1)
                t.matmul(out=psS2[0:8, 0:512], lhsT=gw_p.ap[:, md * NE:(md + 1) * NE],
                         rhs=txf.ap[:, 3:515], start=st, stop=sp, skip_group_check=True)
                return t.matmul(out=psS2[0:8, 512:1024], lhsT=gw_p.ap[:, md * NE:(md + 1) * NE],
                                rhs=txf.ap[:, 515:1027], start=st, stop=sp, skip_group_check=True)
            engine_op("tensor", "tC", gmm, writes=[pS2] if md == MD - 1 else [],
                      reads=[txf, gw_p],
                      extra_waits=(B.pre_write(pS2) if md == 0 else []))

        engine_op("vector", "vC",
                  lambda v: v.tensor_copy(out=glog.ap[0:8, 0:1024], in_=psS2[0:8, 0:1024]),
                  writes=[glog], reads=[pS2])

        for c in range(8):
            engine_op("tensor", "tC",
                      lambda t, c=c: t.transpose(out=psA[:, 0:8], in_=glog.ap[0:8, c * P:(c + 1) * P],
                                                 identity=eye_f.ap[0:8, 0:8]),
                      writes=[pA], reads=[glog, eye_f])

            engine_op("vector", "vC",
                      lambda v: v.tensor_copy(out=gtok.ap, in_=psA[:, 0:8]),
                      writes=[gtok], reads=[pA])
            engine_op("vector", "vC",
                      lambda v: v.reduce_max(out=m1.ap, in_=gtok.ap,
                                             axis=mybir.AxisListType.X),
                      writes=[m1], reads=[gtok])
            engine_op("vector", "vC",
                      lambda v: v.tensor_single_scalar(out=mask1.ap, in_=gtok.ap,
                                                       scalar=m1.ap, op=ALU.is_ge),
                      writes=[mask1], reads=[gtok, m1])
            engine_op("vector", "vC",
                      lambda v: v.scalar_tensor_tensor(out=l2s.ap, in0=mask1.ap,
                                                       scalar=-1e30, in1=gtok.ap,
                                                       op0=ALU.mult, op1=ALU.add),
                      writes=[l2s], reads=[mask1, gtok])
            engine_op("vector", "vC",
                      lambda v: v.reduce_max(out=m2.ap, in_=l2s.ap,
                                             axis=mybir.AxisListType.X),
                      writes=[m2], reads=[l2s])
            engine_op("vector", "vC",
                      lambda v: v.tensor_single_scalar(out=d12.ap, in_=m1.ap,
                                                       scalar=m2.ap, op=ALU.subtract),
                      writes=[d12], reads=[m1, m2])
            engine_op("scalar", "aC",
                      lambda s: s.activation(out=g1.ap, in_=d12.ap, func=AF.Sigmoid),
                      writes=[g1], reads=[d12])
            engine_op("vector", "vC",
                      lambda v: v.tensor_scalar(out=g2.ap, in0=g1.ap, scalar1=-1.0,
                                                scalar2=1.0, op0=ALU.mult, op1=ALU.add),
                      writes=[g2], reads=[g1])
            engine_op("vector", "vC",
                      lambda v: v.tensor_single_scalar(out=l2s.ap, in_=l2s.ap,
                                                       scalar=m2.ap, op=ALU.is_equal),
                      writes=[l2s], reads=[l2s, m2])
            engine_op("vector", "vC",
                      lambda v: v.tensor_single_scalar(out=l2s.ap, in_=l2s.ap,
                                                       scalar=g2.ap, op=ALU.mult),
                      writes=[l2s], reads=[l2s, g2])
            engine_op("vector", "vC",
                      lambda v: v.scalar_tensor_tensor(out=gasm.ap, in0=mask1.ap,
                                                       scalar=g1.ap, in1=l2s.ap,
                                                       op0=ALU.mult, op1=ALU.add),
                      writes=[gasm], reads=[g1, mask1, l2s])
            engine_op("tensor", "tC",
                      lambda t, c=c: t.transpose(out=psB[0:8, 0:P], in_=gasm.ap, identity=eye_f.ap),
                      writes=[pB], reads=[gasm, eye_f])
            engine_op("vector", "vC",
                      lambda v, c=c: v.tensor_copy(out=gatesT.ap[0:8, c * P:(c + 1) * P],
                                                   in_=psB[0:8, 0:P]),
                      writes=[gatesT] if c == 7 else [], reads=[pB])

        engine_op("vector", "vC",
                  lambda v: v.tensor_copy(out=gatesTb.ap[0:8, :], in_=gatesT.ap[0:8, :]),
                  writes=[gatesTb], reads=[gatesT])
        gsp_tok = sync_dma([], [(gates_dr.ap(), gatesTb.ap[0:8, :])], reads=[gatesTb])
        for e in range(NE):
            gsrc = gates_dr.ap()[e:e + 1, :]
            g_rep = bass.AP(gsrc.tensor, gsrc.offset, [[0, P], [1, N_TOK]])
            sync_dma([gbc[e]], [(gbc[e].ap, g_rep)], extra_waits=[gsp_tok])

        sw1r = sw1_h.rearrange("(kc p) m -> kc p m", p=P)
        sw2r = sw2_h.rearrange("(kc p) m -> kc p m", p=P)
        w1r = w1_h.rearrange("e (kc p) m -> e kc p m", p=P)
        w2r = w2_h.rearrange("e (kc p) m -> e kc p m", p=P)

        for half in range(2):
            tok0 = 3 + half * 512
            for hm in range(SHM):
                wdst, wb = stream_w(
                    sw1r[:, :, hm * P:(hm + 1) * P].rearrange("kc p m -> p kc m"), MD * P, 4)
                ps = pAB[hm % 2]

                psap = psA if (hm % 2 == 0) else psB

                def mmsh(t, wdst=wdst, tok0=tok0, psap=psap):
                    for kc in range(MD):
                        ins = t.matmul(out=psap[:, 0:512],
                                       lhsT=wdst[:, kc * P:(kc + 1) * P],
                                       rhs=xn[kc].ap[:, tok0:tok0 + 512],
                                       start=(kc == 0), stop=(kc == MD - 1))
                    return ins
                engine_op("tensor", "tC", mmsh, writes=[ps],
                          reads=[*wb, *xn])
                engine_op("scalar", "aC",
                          lambda s, hm=hm: s.activation(
                              out=shch[hm].ap, in_=(psA if (hm % 2 == 0) else psB)[:, 0:512],
                              func=AF.Gelu, bias=sb1_p.ap[:, hm:hm + 1]),
                          writes=[shch[hm]], reads=[ps, sb1_p])
            for md in range(MD):
                wdst, wb = stream_w(
                    sw2r[:, :, md * P:(md + 1) * P].rearrange("kc p m -> p kc m"), SHM * P, 4)
                ps = pAB[md % 2]
                psap = psA if (md % 2 == 0) else psB

                def mmrt(t, md=md, wdst=wdst, half=half, psap=psap):
                    for kc in range(SHM):
                        t.matmul(out=psap[:, 0:512], lhsT=wdst[:, kc * P:(kc + 1) * P],
                                 rhs=shch[kc].ap, start=(kc == 0), stop=False)
                    return t.matmul(out=psap[:, 0:512], lhsT=B2S[0:8, md * P:(md + 1) * P],
                                    rhs=gatesTb.ap[0:8, half * 512:(half + 1) * 512],
                                    start=False, stop=True)
                engine_op("tensor", "tC", mmrt, writes=[ps],
                          reads=[*wb, *shch, b2_b, gatesTb])
                engine_op("scalar", "aC",
                          lambda s, md=md, half=half, psap=psap: s.add(
                              out=routed[md].ap[:, half * 512:(half + 1) * 512],
                              in_=psap[:, 0:512], add=sb2_p.ap[:, md:md + 1]),
                          writes=[routed[md]] if half == 1 else [], reads=[ps, sb2_p])

        for e in range(NE):
            for hm in range(HM):
                wdst, wb = stream_w(
                    w1r[e, :, :, hm * P:(hm + 1) * P].rearrange("kc p m -> p kc m"), MD * P, 4)
                for piece in range(2):
                    ps = pAB[piece]
                    psap = psA if piece == 0 else psB
                    tok0 = 3 + piece * 512

                    def mmh(t, wdst=wdst, tok0=tok0, psap=psap):
                        for kc in range(MD):
                            ins = t.matmul(out=psap[:, 0:512], lhsT=wdst[:, kc * P:(kc + 1) * P],
                                           rhs=xn[kc].ap[:, tok0:tok0 + 512],
                                           start=(kc == 0), stop=(kc == MD - 1))
                        return ins
                    engine_op("tensor", "tC", mmh, writes=[ps], reads=[*wb, *xn])
                    engine_op("scalar", "aC",
                              lambda s, e=e, hm=hm, piece=piece, psap=psap: s.activation(
                                  out=hch[hm].ap[:, piece * 512:(piece + 1) * 512],
                                  in_=psap[:, 0:512], func=AF.Gelu,
                                  bias=b1_p.ap[:, e * HM + hm:e * HM + hm + 1]),
                              writes=[hch[hm]] if piece == 1 else [], reads=[ps, b1_p])

            def hscale(v, e=e):
                for hm in range(HM):
                    ins = v.tensor_mul(out=hch[hm].ap, in0=hch[hm].ap, in1=gbc[e].ap)
                return ins
            engine_op("vector", "vC", hscale, writes=list(hch),
                      reads=[gbc[e]] + list(hch))

            for md in range(MD):
                wdst, wb = stream_w(
                    w2r[e, :, :, md * P:(md + 1) * P].rearrange("kc p m -> p kc m"), HM * P, 2)
                for piece in range(2):
                    ps = pCD[piece]
                    psap = psC if piece == 0 else psD

                    def mmw2(t, wdst=wdst, piece=piece, psap=psap):
                        for kc in range(HM):
                            ins = t.matmul(out=psap[:, 0:512], lhsT=wdst[:, kc * P:(kc + 1) * P],
                                           rhs=hch[kc].ap[:, piece * 512:(piece + 1) * 512],
                                           start=(kc == 0), stop=(kc == HM - 1))
                        return ins
                    engine_op("tensor", "tC", mmw2, writes=[ps], reads=[*wb] + list(hch))
                    engine_op("vector", "vC",
                              lambda v, md=md, piece=piece, psap=psap: v.tensor_add(
                                  out=routed[md].ap[:, piece * 512:(piece + 1) * 512],
                                  in0=routed[md].ap[:, piece * 512:(piece + 1) * 512],
                                  in1=psap[:, 0:512]),
                              writes=[routed[md]] if (e == NE - 1 and piece == 1) else [],
                              reads=[ps, routed[md]])

        ffn_r = ffn_d.ap().rearrange("(c p) n -> c p n", p=P)
        for md in range(MD):
            sync_dma([], [(ffn_r[md, :, :], routed[md].ap)], reads=[routed[md]])

        winr = w_in.rearrange("(kc p) m -> kc p m", p=P)
        wxpr = w_xp.rearrange("(kc p) m -> kc p m", p=P)
        xtdr = xt_d.ap().rearrange("(c p) n -> c p n", p=P)
        gspr = gate_d.ap().rearrange("(c p) n -> c p n", p=P)
        xtb = [AR2s[0], AR2s[1]]
        gtb = [AR2s[2], AR2s[3]]
        ssm1b = AR2s[4]

        m1_spill_toks = []
        for m in range(2 * DI):
            is_x = m < DI
            dc = m if is_x else m - DI
            pieces = PIECES3 if is_x else [(3, 512), (515, 512)]
            wdst, wb = stream_w(
                winr[:, :, m * P:(m + 1) * P].rearrange("kc p m -> p kc m"), MD * P, 4)
            pslist = [pA, pB, pC][:len(pieces)]
            psaps = [psA, psB, psC][:len(pieces)]

            def mmip(t, wdst=wdst, pieces=pieces, psaps=psaps):
                for pi, (c0, w) in enumerate(pieces):
                    for kc in range(MD):
                        ins = t.matmul(out=psaps[pi][:, 0:w], lhsT=wdst[:, kc * P:(kc + 1) * P],
                                       rhs=xn[kc].ap[:, c0:c0 + w],
                                       start=(kc == 0), stop=(kc == MD - 1))
                return ins
            engine_op("tensor", "tC", mmip, writes=pslist, reads=[*wb, *xn])

            if is_x:
                bx = xf[dc % 2]

                def evx(s, bx=bx, pieces=pieces, psaps=psaps):
                    for pi, (c0, w) in enumerate(pieces):
                        ins = s.copy(out=bx.ap[:, c0:c0 + w], in_=psaps[pi][:, 0:w])
                    return ins
                engine_op("scalar", "aC", evx, writes=[bx], reads=pslist)

                engine_op("vector", "vC",
                          lambda v, bx=bx, dc=dc: v.tensor_single_scalar(
                              out=x2f.ap[:, 0:N_TOK], in_=bx.ap[:, 0:N_TOK],
                              scalar=convw_p.ap[:, dc * 4:dc * 4 + 1], op=ALU.mult),
                          writes=[x2f], reads=[bx, convw_p])
                engine_op("vector", "vC",
                          lambda v, bx=bx, dc=dc: v.scalar_tensor_tensor(
                              out=txf.ap[:, 0:N_TOK], in0=bx.ap[:, 1:1 + N_TOK],
                              scalar=convw_p.ap[:, dc * 4 + 1:dc * 4 + 2],
                              in1=x2f.ap[:, 0:N_TOK], op0=ALU.mult, op1=ALU.add),
                          writes=[txf], reads=[bx, x2f, convw_p])
                engine_op("vector", "vC",
                          lambda v, bx=bx, dc=dc: v.scalar_tensor_tensor(
                              out=x2f.ap[:, 0:N_TOK], in0=bx.ap[:, 2:2 + N_TOK],
                              scalar=convw_p.ap[:, dc * 4 + 2:dc * 4 + 3],
                              in1=txf.ap[:, 0:N_TOK], op0=ALU.mult, op1=ALU.add),
                          writes=[x2f], reads=[bx, txf, convw_p])
                engine_op("vector", "vC",
                          lambda v, bx=bx, dc=dc: v.scalar_tensor_tensor(
                              out=txf.ap[:, 0:N_TOK], in0=bx.ap[:, 3:3 + N_TOK],
                              scalar=convw_p.ap[:, dc * 4 + 3:dc * 4 + 4],
                              in1=x2f.ap[:, 0:N_TOK], op0=ALU.mult, op1=ALU.add),
                          writes=[txf], reads=[bx, x2f, convw_p])

                xb = xtb[dc % 2]
                engine_op("scalar", "aC",
                          lambda s, dc=dc, xb=xb: s.activation(
                              out=xb.ap, in_=txf.ap[:, 0:N_TOK], func=AF.Silu,
                              bias=convb_p.ap[:, dc:dc + 1]),
                          writes=[xb], reads=[txf, convb_p])
                m1_spill_toks.append(
                    sync_dma([], [(xtdr[dc, :, :], xb.ap)], reads=[xb]))

                wxdst, wxb = stream_w(wxpr[dc, :, :], 160, 1)

                def mmxp(t, dc=dc, wxdst=wxdst, xb=xb):
                    st, sp = (dc == 0), (dc == DI - 1)
                    t.matmul(out=psS1[:, 0:512], lhsT=wxdst[:, 0:128], rhs=xb.ap[:, 0:512], start=st, stop=sp, skip_group_check=True)
                    t.matmul(out=psS1[:, 512:1024], lhsT=wxdst[:, 0:128], rhs=xb.ap[:, 512:1024], start=st, stop=sp, skip_group_check=True)
                    t.matmul(out=psS2[:, 0:512], lhsT=wxdst[:, 128:160], rhs=xb.ap[:, 0:512], start=st, stop=sp, skip_group_check=True)
                    return t.matmul(out=psS2[:, 512:1024], lhsT=wxdst[:, 128:160], rhs=xb.ap[:, 512:1024], start=st, stop=sp, skip_group_check=True)
                engine_op("tensor", "tC", mmxp,
                          writes=[pS1, pS2] if dc == DI - 1 else [], reads=[*wxb, xb],
                          extra_waits=((B.pre_write(pS1) + B.pre_write(pS2)) if dc == 0 else []))
            else:
                gb = gtb[dc % 2]

                def evg(s, gb=gb, pieces=pieces, psaps=psaps):
                    for pi, (c0, w) in enumerate(pieces):
                        ins = s.copy(out=gb.ap[:, (c0 - 3):(c0 - 3) + w], in_=psaps[pi][:, 0:w])
                    return ins
                engine_op("scalar", "aC", evg, writes=[gb], reads=pslist)
                m1_spill_toks.append(
                    sync_dma([], [(gspr[dc, :, :], gb.ap)], reads=[gb]))

        def evssm(s):
            s.copy(out=ssm1b.ap, in_=psS1[:, 0:N_TOK])
            return s.copy(out=ssm32.ap, in_=psS2[:, 0:N_TOK])
        engine_op("scalar", "aC", evssm, writes=[ssm1b, ssm32], reads=[pS1, pS2])
        m1_spill_toks.append(
            sync_dma([], [(ssm_d.ap()[0:128, :], ssm1b.ap), (ssm_d.ap()[128:160, :], ssm32.ap)],
                     reads=[ssm1b, ssm32]))

        c1_wait = B.wait_toks(m1_spill_toks)

        def coll1(gpsimd):
            c1_wait(gpsimd)
            gpsimd.collective_compute(
                "AllGather", ALU.bypass, replica_groups=[list(range(NCORES))],
                ins=[ssm_d.ap().opt()], outs=[gssm.ap().opt()]).then_inc(gC, 1)
            gpsimd.collective_compute(
                "AllGather", ALU.bypass, replica_groups=[list(range(NCORES))],
                ins=[xt_d.ap().opt()], outs=[gxt.ap().opt()]).then_inc(gC, 1)
        B.on("gpsimd", coll1)
        B.tick("gC", 2)
        coll1_tok = ("gC", B.cnt["gC"])

        gxtr = gxt.ap().rearrange("(s b p) n -> b s p n", b=NCORES, p=512)
        gssmr = gssm.ap().rearrange("(r q) n -> r q n", q=160)
        ybr = y_d.ap().rearrange("(c p) (g n) -> c p g n", p=P, n=TSEG)
        a_b = [AR2s[0], AR2s[1]]
        u_b = [AR2s[2], AR2s[3]]
        bb_b = [AR2s[4], AR2s[5]]
        cb_b = [AR2s[6], AR2s[7]]
        ts_b = AR2s[8]
        xts_b = [AR2s[9 + i] for i in range(OC)]
        dtx_b = [AR2s[13], AR2s[14]]
        yev_b = AR2s[15]

        engine_op("vector", "vC", lambda v: v.memset(carry.ap, 0.0), writes=[carry])

        xts_all = AR2[:, 9 * N_TOK:13 * N_TOK].rearrange("p (c n) -> p c n", c=OC)
        gxtr_p = gxt.ap().rearrange("(s b c p) n -> b s p c n", b=NCORES, c=OC, p=P)
        for seg in range(NSEG):
            sync_dma([ts_b], [(ts_b.ap, gssmr[seg, 0:128, :])], extra_waits=[coll1_tok])
            wfrag = B.wait_toks(
                sum((B.pre_write(xb) for xb in xts_b), []) + [coll1_tok])
            sn_ldxt = B.skey("ldxt")
            semh_ldxt = B.sems[sn_ldxt]

            def ldxt(sync, seg=seg, wfrag=wfrag):
                wfrag(sync)
                sync.dma_start(
                    out=xts_all,
                    in_=gxtr_p[bass.ts(regbox["rank"], 1), seg, :, :, :].opt(),
                ).then_inc(semh_ldxt, 16)
            B.on("sync", ldxt)
            xtok = B.tick(sn_ldxt, 16)
            for xb in xts_b:
                B.post_write(xtok, xb)
            for dc in range(OC):
                xbuf = xts_b[dc]

                def mmdt(t, dc=dc):
                    t.matmul(out=psA[:, 0:512], lhsT=wdt_p.ap[:, dc * P:(dc + 1) * P],
                             rhs=ts_b.ap[:, 0:512], start=True, stop=True)
                    return t.matmul(out=psB[:, 0:512], lhsT=wdt_p.ap[:, dc * P:(dc + 1) * P],
                                    rhs=ts_b.ap[:, 512:1024], start=True, stop=True)
                engine_op("tensor", "tC", mmdt, writes=[pA, pB], reads=[ts_b, wdt_p])

                engine_op("scalar", "aC",
                          lambda s, dc=dc: (
                              s.add(out=spA.ap[:, 0:512], in_=psA[:, 0:512],
                                    add=dtb_p.ap[:, dc:dc + 1]),
                              s.add(out=spA.ap[:, 512:1024], in_=psB[:, 0:512],
                                    add=dtb_p.ap[:, dc:dc + 1]))[-1],
                          writes=[spA], reads=[pA, pB, dtb_p])
                engine_op("scalar", "aC",
                          lambda s: s.activation(out=spB.ap, in_=spA.ap, func=AF.Abs),
                          writes=[spB], reads=[spA])
                engine_op("scalar", "aC",
                          lambda s: s.activation(out=spC.ap, in_=spB.ap, func=AF.Exp, scale=-1.0),
                          writes=[spC], reads=[spB])
                engine_op("scalar", "aC",
                          lambda s: s.activation(out=spB.ap, in_=spC.ap, func=AF.Ln, bias=1.0),
                          writes=[spB], reads=[spC])

                dxb = dtx_b[dc % 2]

                engine_op("vector", "vC",
                          lambda v: v.scalar_tensor_tensor(out=dt_f.ap, in0=spA.ap, scalar=0.0,
                                                           in1=spB.ap, op0=ALU.max, op1=ALU.add),
                          writes=[dt_f], reads=[spA, spB])
                engine_op("vector", "vC",
                          lambda v, dxb=dxb, xbuf=xbuf: v.tensor_mul(out=dxb.ap, in0=dt_f.ap,
                                                                     in1=xbuf.ap),
                          writes=[dxb], reads=[dt_f, xbuf])

                for s_i in range(S):
                    pp = s_i % 2
                    bsrc = gssmr[seg, 128 + s_i:129 + s_i, :]
                    b_ap = bass.AP(bsrc.tensor, bsrc.offset, [[0, P], [1, TSEG]])
                    csrc = gssmr[seg, 144 + s_i:145 + s_i, :]
                    c_ap = bass.AP(csrc.tensor, csrc.offset, [[0, P], [1, TSEG]])
                    sync_dma([bb_b[pp], cb_b[pp]],
                             [(bb_b[pp].ap, b_ap), (cb_b[pp].ap, c_ap)],
                             extra_waits=[coll1_tok])
                    engine_op("scalar", "aC",
                              lambda s, dc=dc, s_i=s_i, pp=pp: s.activation(
                                  out=a_b[pp].ap, in_=dt_f.ap, func=AF.Exp,
                                  scale=A_p.ap[:, dc * S + s_i:dc * S + s_i + 1]),
                              writes=[a_b[pp]], reads=[dt_f, A_p])

                    engine_op("vector", "vC",
                              lambda v, pp=pp, dxb=dxb: v.tensor_mul(
                                  out=u_b[pp].ap, in0=dxb.ap, in1=bb_b[pp].ap),
                              writes=[u_b[pp]], reads=[dxb, bb_b[pp]])
                    ci = dc * S + s_i

                    def scano(v, seg=seg, s_i=s_i, pp=pp, ci=ci):
                        init = 0.0 if seg == 0 else carry.ap[:, ci:ci + 1]
                        return v.tensor_tensor_scan(out=state[s_i].ap, data0=a_b[pp].ap,
                                                    data1=u_b[pp].ap, initial=init,
                                                    op0=ALU.mult, op1=ALU.add)
                    engine_op("vector", "vC", scano,
                              writes=[state[s_i]], reads=[a_b[pp], u_b[pp], carry])
                    engine_op("vector", "vC",
                              lambda v, s_i=s_i, ci=ci: v.tensor_copy(
                                  out=carry.ap[:, ci:ci + 1],
                                  in_=state[s_i].ap[:, TSEG - 1:TSEG]),
                              writes=[carry], reads=[state[s_i], carry])
                    engine_op("vector", "vC",
                              lambda v, s_i=s_i, pp=pp: v.tensor_mul(
                                  out=state[s_i].ap, in0=state[s_i].ap, in1=cb_b[pp].ap),
                              writes=[state[s_i]], reads=[state[s_i], cb_b[pp]])

                for tc2 in range(2):
                    psap = psC if tc2 == 0 else psD

                    def yred(t, tc2=tc2, psap=psap):
                        for s_i in range(S):
                            ins = t.matmul(out=psap[:, 0:512], lhsT=eye_b.ap,
                                           rhs=state[s_i].ap[:, tc2 * 512:(tc2 + 1) * 512],
                                           start=(s_i == 0), stop=(s_i == S - 1))
                        return ins
                    engine_op("tensor", "tC", yred, writes=[pCD[tc2]],
                              reads=list(state) + [eye_b])
                    engine_op("scalar", "aC",
                              lambda s, tc2=tc2, psap=psap: s.copy(
                                  out=yev_b.ap[:, tc2 * 512:(tc2 + 1) * 512], in_=psap[:, 0:512]),
                              writes=[yev_b], reads=[pCD[tc2]])
                m2_tok = sync_dma([], [(ybr[dc, :, seg, :], yev_b.ap)], reads=[yev_b])

        c2_wait = B.wait_toks([m2_tok])

        def coll2(gpsimd):
            c2_wait(gpsimd)
            gpsimd.collective_compute(
                "AllGather", ALU.bypass, replica_groups=[list(range(NCORES))],
                ins=[y_d.ap().opt()], outs=[gy.ap().opt()]).then_inc(gC, 1)
        B.on("gpsimd", coll2)
        B.tick("gC", 1)
        coll2_tok = ("gC", B.cnt["gC"])

        gyr = gy.ap().rearrange("(b p) (r h n) -> p b r h n", p=P, r=NCORES, n=512)
        xtdr2 = xt_d.ap().rearrange("(c p) (h n) -> c p h n", p=P, n=512)
        gspr2 = gate_d.ap().rearrange("(c p) (h n) -> c p h n", p=P, n=512)
        fspr2 = ffn_d.ap().rearrange("(c p) (h n) -> c p h n", p=P, n=512)
        wor = w_out.rearrange("(kc p) m -> kc p m", p=P)
        outr = out_t.rearrange("(c p) (h n) -> c p h n", p=P, n=512)
        ytk = [Buf(AR2[:, i * 512:(i + 1) * 512], f"ytk{i}") for i in range(8)]
        xtk = [AR2s[8], AR2s[9]]
        gak = [AR2s[10], AR2s[11]]
        sgk = [AR2s[12], AR2s[13]]
        ffk = [AR2s[14], AR2s[15]]

        out_store_toks = []
        for half in range(2):
            for dc in range(DI):
                pp = dc % 2
                yb, xb, gb, sb = ytk[dc % 8], xtk[pp], gak[pp], sgk[pp]
                if dc % 8 == 0:
                    grp = [ytk[i] for i in range(8)]
                    ydst = AR2[:, 0:8 * 512].rearrange("p (c n) -> p c n", c=8)
                    wfrag = B.wait_toks(
                        sum((B.pre_write(g_) for g_ in grp), []) + [coll2_tok])
                    sn_ldy = B.skey("ldy")
                    semh_ldy = B.sems[sn_ldy]

                    def ldy(sync, wfrag=wfrag, dc0=dc, half=half, ydst=ydst):
                        wfrag(sync)
                        sync.dma_start(
                            out=ydst,
                            in_=gyr[:, dc0:dc0 + 8, bass.ts(regbox["rank"], 1), half, :].opt(),
                        ).then_inc(semh_ldy, 16)
                    B.on("sync", ldy)
                    ytok = B.tick(sn_ldy, 16)
                    for g_ in grp:
                        B.post_write(ytok, g_)
                sync_dma([xb], [(xb.ap[:, 0:512], xtdr2[dc, :, half, :])], extra_waits=[coll2_tok])
                sync_dma([gb], [(gb.ap[:, 0:512], gspr2[dc, :, half, :])], extra_waits=[coll2_tok])
                engine_op("scalar", "aC",
                          lambda s, gb=gb, sb=sb: s.activation(out=sb.ap[:, 0:512],
                                                               in_=gb.ap[:, 0:512], func=AF.Silu),
                          writes=[sb], reads=[gb])

                engine_op("vector", "vC",
                          lambda v, dc=dc, pp=pp, yb=yb, xb=xb: v.scalar_tensor_tensor(
                              out=yfl[pp].ap, in0=xb.ap[:, 0:512],
                              scalar=D_p.ap[:, dc:dc + 1], in1=yb.ap,
                              op0=ALU.mult, op1=ALU.add),
                          writes=[yfl[pp]], reads=[xb, yb, D_p])
                engine_op("vector", "vC",
                          lambda v, dc=dc, pp=pp, sb=sb: v.tensor_mul(
                              out=ym[dc].ap, in0=yfl[pp].ap, in1=sb.ap[:, 0:512]),
                          writes=[ym[dc]], reads=[yfl[pp], sb])

            for md in range(MD):
                wdst, wb = stream_w(
                    wor[:, :, md * P:(md + 1) * P].rearrange("kc p m -> p kc m"), DI * P, 8)
                ps = pAB[md % 2]
                psap = psA if md % 2 == 0 else psB

                def mmop(t, wdst=wdst, psap=psap):
                    for kc in range(DI):
                        ins = t.matmul(out=psap[:, 0:512], lhsT=wdst[:, kc * P:(kc + 1) * P],
                                       rhs=ym[kc].ap, start=(kc == 0), stop=(kc == DI - 1))
                    return ins
                engine_op("tensor", "tC", mmop, writes=[ps], reads=[*wb] + list(ym))
                xrb = xre[md % 2]
                frb = ffk[md % 2]
                sync_dma([xrb], [(xrb.ap, xr[md, :, HALO + half * 512:HALO + half * 512 + 512])])
                sync_dma([frb], [(frb.ap[:, 0:512], fspr2[md, :, half, :])])

                engine_op("vector", "vC",
                          lambda v, psap=psap, xrb=xrb, pp2=md % 2: v.tensor_add(
                              out=oev[pp2].ap, in0=psap[:, 0:512], in1=xrb.ap),
                          writes=[oev[md % 2]], reads=[ps, xrb])
                engine_op("vector", "vC",
                          lambda v, frb=frb, pp2=md % 2: v.tensor_add(
                              out=oev[pp2].ap, in0=oev[pp2].ap, in1=frb.ap[:, 0:512]),
                          writes=[oev[md % 2]], reads=[oev[md % 2], frb])
                out_store_toks.append(
                    sync_dma([], [(outr[md, :, half, :], oev[md % 2].ap)], reads=[oev[md % 2]]))

        fin_wait = B.wait_toks(out_store_toks)

        def fin(sync):
            fin_wait(sync)
        B.on("sync", fin)

        @block.sync
        def _(sync):
            for fn in B.sched["sync"]:
                fn(sync)

        @block.vector
        def _(vector):
            for fn in B.sched["vector"]:
                fn(vector)

        @block.scalar
        def _(scalar):
            for fn in B.sched["scalar"]:
                fn(scalar)

        @block.tensor
        def _(tensor):
            for fn in B.sched["tensor"]:
                fn(tensor)

        @block.gpsimd
        def _(gpsimd):
            for fn in B.sched["gpsimd"]:
                fn(gpsimd)

    return nc


_NC_CACHE = {}


def kernel(**inputs):
    ins = {k: np.asarray(v) for k, v in inputs.items()}
    x = ins["L0_attn_hidden_states"][0]

    if "nc" not in _NC_CACHE:
        _NC_CACHE["nc"] = build_program()
    nc = _NC_CACHE["nc"]

    eye = np.eye(P, dtype=np.float32)
    w_in_a = np.ascontiguousarray(ins["L0_attn_in_proj_w"].T)
    w_xp_a = np.ascontiguousarray(ins["L0_attn_x_proj_w"].T)
    w_out_a = np.ascontiguousarray(ins["L0_attn_out_proj_w"].T)
    convw = np.ascontiguousarray(
        ins["L0_attn_conv_w"].reshape(DI, P, 4).transpose(1, 0, 2).reshape(P, DI * 4))
    convb = np.ascontiguousarray(ins["L0_attn_conv_b"].reshape(DI, P).T)
    D_v = np.ascontiguousarray(ins["L0_attn_D"].reshape(DI, P).T)
    gate_w = np.ascontiguousarray(ins["L0_ffn_gate_w"])
    w1 = np.ascontiguousarray(ins["L0_ffn_w1"])
    b1 = np.ascontiguousarray(
        ins["L0_ffn_b1"].reshape(NE, HM, P).transpose(2, 0, 1).reshape(P, NE * HM))
    w2 = np.ascontiguousarray(ins["L0_ffn_w2"])
    b2 = np.ascontiguousarray(ins["L0_ffn_b2"])
    sw1 = np.ascontiguousarray(ins["L0_ffn_shared_w1"])
    sb1 = np.ascontiguousarray(ins["L0_ffn_shared_b1"].reshape(SHM, P).T)
    sw2 = np.ascontiguousarray(ins["L0_ffn_shared_w2"])
    sb2 = np.ascontiguousarray(ins["L0_ffn_shared_b2"].reshape(MD, P).T)
    w_dt_full = np.ascontiguousarray(ins["L0_attn_dt_proj_w"].T)

    in_maps = []
    for c in range(NCORES):
        t0 = c * N_TOK
        xs = np.zeros((NH, D_MODEL), np.float32)
        if c == 0:
            xs[HALO:] = x[0:N_TOK]
        else:
            xs[:] = x[t0 - HALO:t0 + N_TOK]
        ch0 = 512 * c
        A_sl = np.ascontiguousarray(
            ins["L0_attn_A"][ch0:ch0 + 512].reshape(OC, P, S).transpose(1, 0, 2).reshape(P, OC * S))
        dtb = np.ascontiguousarray(
            ins["L0_attn_dt_proj_b"][ch0:ch0 + 512].reshape(OC, P).T)
        wdt = np.ascontiguousarray(w_dt_full[:, ch0:ch0 + 512])
        in_maps.append({
            "x_t": np.ascontiguousarray(xs.T),
            "halo_mask": np.full((P, 1), 0.0 if c == 0 else 1.0, np.float32),
            "rank": np.array([[c]], np.int32),
            "eye": eye,
            "w_in": w_in_a, "w_xp": w_xp_a, "w_dt": wdt, "dt_b": dtb, "A_sl": A_sl,
            "w_out": w_out_a, "conv_w": convw, "conv_b": convb, "D_v": D_v,
            "gate_w": gate_w, "w1": w1, "b1": b1, "w2": w2, "b2": b2,
            "sw1": sw1, "sb1": sb1, "sw2": sw2, "sb2": sb2,
        })

    res = run_bass_kernel_spmd(nc, in_maps, core_ids=list(range(NCORES)))
    out = np.empty((1, T_ALL, D_MODEL), np.float32)
    for c in range(NCORES):
        out[0, c * N_TOK:(c + 1) * N_TOK, :] = res.results[c]["out_t"].T
    return out
